# version 01
# baseline (speedup 1.0000x reference)
import numpy as np
import ml_dtypes
import concourse.bass as bass
import concourse.mybir as mybir
from concourse.bass_utils import run_bass_kernel_spmd

F32 = mybir.dt.float32
BF16 = mybir.dt.bfloat16
I32 = mybir.dt.int32
ALU = mybir.AluOpType
ACT = mybir.ActivationFunctionType

INTERLEAVE = True
B_CORE = 2
S = 2048
D = 1024
DC = 8
H = 3072
HC = 24
NL = 64
T = 512
NT = S // T
NSUB = S // 128
EPS = 1e-5
ENGS = ("sync", "tensor", "vector", "scalar", "gpsimd")


class Sched:
    def __init__(self):
        self.ops = {e: [] for e in ENGS}
        self.semv = {}
        self.last_wait = {}

    def do(self, eng, fn, sem=None, inc=1, dma=False):
        if sem is None:
            self.ops[eng].append(fn)
            return None
        if dma:
            inc = 16
        self.ops[eng].append(lambda e, fn=fn, sem=sem, inc=inc: fn(e).then_inc(sem, inc))
        v = self.semv.get(sem, 0) + inc
        self.semv[sem] = v
        return (sem, v, "dma" if dma else eng)

    def wait(self, eng, tok):
        if tok is None:
            return
        sem, v, prod = tok
        if prod == eng:
            return
        key = (eng, sem)
        if self.last_wait.get(key, -1) >= v:
            return
        self.last_wait[key] = v
        self.ops[eng].append(lambda e, sem=sem, v=v: e.wait_ge(sem, v))

    def emit(self, block):
        for eng in ENGS:
            ops = self.ops[eng]
            def body(e, ops=ops):
                for fn in ops:
                    fn(e)
            getattr(block, eng)(body)


def host_prepare(inputs):
    out = {}
    wg = np.asarray(inputs["L0_ffn_w_gate"])
    wu = np.asarray(inputs["L0_ffn_w_up"])
    wd = np.asarray(inputs["L0_ffn_w_down"])
    out["wg_t"] = np.ascontiguousarray(wg.reshape(HC, 128, DC, 128).transpose(0, 3, 2, 1)).reshape(HC, 128, DC * 128)
    out["wu_t"] = np.ascontiguousarray(wu.reshape(HC, 128, DC, 128).transpose(0, 3, 2, 1)).reshape(HC, 128, DC * 128)
    out["wd_t"] = np.ascontiguousarray(wd.reshape(DC, 128, HC, 128).transpose(0, 3, 2, 1)).reshape(DC, 128, HC * 128)
    for nm, w in (("wi_t", "L0_attn_Wi"), ("wf_t", "L0_attn_Wf"),
                  ("wz_t", "L0_attn_Wz"), ("wo_t", "L0_attn_Wo")):
        W = np.asarray(inputs[w])
        out[nm] = np.ascontiguousarray(W.reshape(NL, DC, 128).transpose(1, 2, 0))
    Wp = np.asarray(inputs["L0_attn_Wp"])
    out["wp_t"] = np.ascontiguousarray(Wp.reshape(DC, 128, NL).transpose(0, 2, 1))
    for nm, k in (("d_bi", "L0_attn_bi"), ("d_bf", "L0_attn_bf"),
                  ("d_bz", "L0_attn_bz"), ("d_bo", "L0_attn_bo")):
        out[nm] = np.ascontiguousarray(np.asarray(inputs[k]).reshape(NL, 1))
    for nm, k in (("d_bp", "L0_attn_bp"), ("d_lnw", "L0_attn_ln_w"), ("d_lnb", "L0_attn_ln_b")):
        out[nm] = np.ascontiguousarray(np.asarray(inputs[k]).reshape(DC, 128).T)
    return out


def build_program(n_batches=B_CORE):
    nc = bass.Bass("TRN2", target_bir_lowering=False, debug=False)

    x_in = nc.dram_tensor("x_in", [B_CORE, S, D], F32, kind="ExternalInput").ap()
    y_out = nc.dram_tensor("y_out", [B_CORE, S, D], F32, kind="ExternalOutput").ap()
    wg_t = nc.dram_tensor("wg_t", [HC, 128, DC * 128], I32, kind="ExternalInput").ap()
    wu_t = nc.dram_tensor("wu_t", [HC, 128, DC * 128], I32, kind="ExternalInput").ap()
    wd_t = nc.dram_tensor("wd_t", [DC, 128, HC * 128], I32, kind="ExternalInput").ap()
    wi_t = nc.dram_tensor("wi_t", [DC, 128, NL], F32, kind="ExternalInput").ap()
    wf_t = nc.dram_tensor("wf_t", [DC, 128, NL], F32, kind="ExternalInput").ap()
    wz_t = nc.dram_tensor("wz_t", [DC, 128, NL], F32, kind="ExternalInput").ap()
    wo_t = nc.dram_tensor("wo_t", [DC, 128, NL], F32, kind="ExternalInput").ap()
    wp_t = nc.dram_tensor("wp_t", [DC, NL, 128], F32, kind="ExternalInput").ap()
    d_bi = nc.dram_tensor("d_bi", [NL, 1], F32, kind="ExternalInput").ap()
    d_bf = nc.dram_tensor("d_bf", [NL, 1], F32, kind="ExternalInput").ap()
    d_bz = nc.dram_tensor("d_bz", [NL, 1], F32, kind="ExternalInput").ap()
    d_bo = nc.dram_tensor("d_bo", [NL, 1], F32, kind="ExternalInput").ap()
    d_bp = nc.dram_tensor("d_bp", [128, DC], F32, kind="ExternalInput").ap()
    d_lnw = nc.dram_tensor("d_lnw", [128, DC], F32, kind="ExternalInput").ap()
    d_lnb = nc.dram_tensor("d_lnb", [128, DC], F32, kind="ExternalInput").ap()
    wgb = nc.dram_tensor("wgb", [HC, 128, DC * 128], BF16, kind="Internal").ap()
    wub = nc.dram_tensor("wub", [HC, 128, DC * 128], BF16, kind="Internal").ap()
    wdb = nc.dram_tensor("wdb", [DC, 128, HC * 128], BF16, kind="Internal").ap()

    R_h = nc.alloc_sbuf_tensor("R", [128, DC * S], F32)
    wslab_g_h = nc.alloc_sbuf_tensor("wslab_g", [128, 2 * DC * 128], BF16)
    wslab_u_h = nc.alloc_sbuf_tensor("wslab_u", [128, 2 * DC * 128], BF16)
    wslab_d_h = nc.alloc_sbuf_tensor("wslab_d", [128, 3 * 8 * 128], BF16)
    sb_b = {}
    for nm in ("i", "f", "z", "o"):
        sb_b[nm] = nc.alloc_sbuf_tensor(f"sb_b{nm}", [NL, 1], F32)
    sb_bp = nc.alloc_sbuf_tensor("sb_bp", [128, DC], F32)
    sb_lnw = nc.alloc_sbuf_tensor("sb_lnw", [128, DC], F32)
    sb_lnb = nc.alloc_sbuf_tensor("sb_lnb", [128, DC], F32)
    ones_col = nc.alloc_sbuf_tensor("ones_col", [1, 128], F32)
    c127 = nc.alloc_sbuf_tensor("c127", [128, 1], F32)
    ones_stat = nc.alloc_sbuf_tensor("ones_stat", [128, 1], F32)
    ident = nc.alloc_sbuf_tensor("ident", [128, 128], F32)
    rows_h = nc.alloc_sbuf_tensor("rows", [1, 4 * T], F32)

    ARENA = 112 * 1024
    base = nc.sbuf_base
    assert base + ARENA <= nc.sbuf_top - 64, (base, nc.sbuf_top)
    off = [base]

    def arena(name, shape, dtype):
        nbytes = int(np.prod(shape[1:])) * mybir.dt.size(dtype)
        t = nc.alloc_sbuf_tensor_at(name, shape, dtype, offset=off[0])
        off[0] += (nbytes + 31) & ~31
        assert off[0] <= base + ARENA, (name, off[0] - base)
        return t

    xsp_h = [arena(f"xsp{i}", [128, DC * T], BF16) for i in range(3)]
    hsp_h = arena("hsp", [128, HC * 3 * T], BF16)
    xn_tmp = arena("xn_tmp", [128, T], F32)
    xsq = arena("xsq", [128, 2 * T], F32)
    g_tmp = arena("g_tmp", [128, T], F32)
    arep = arena("arep", [128, T], F32)
    brep = arena("brep", [128, T], F32)
    h_tmp = arena("h_tmp", [128, T], F32)

    off[0] = base
    iostage = arena("iostage", [128, 2 * 1024], F32)
    off[0] = base + 8 * 1024
    sc_li = arena("sc_li", [NL, S], F32)
    sc_lf = arena("sc_lf", [NL, S], F32)
    sc_z = arena("sc_z", [NL, S], F32)
    sc_o = arena("sc_o", [NL, S], F32)
    sc_m = arena("sc_m", [NL, S + 4], F32)
    xn2 = arena("xn2", [128, 2 * T], F32)
    axsq = arena("axsq", [128, 2 * T], F32)
    aarep = arena("aarep", [128, T], F32)
    abrep = arena("abrep", [128, T], F32)
    aw = {}
    for nm in ("i", "f", "z", "o"):
        aw[nm] = arena(f"aw_{nm}", [128, DC * NL], F32)
    aw_p = arena("aw_p", [NL, DC * 128], F32)
    off[0] = base
    cv_i32 = arena("cv_i32", [128, 2 * 1536], I32)
    cv_bf = arena("cv_bf", [128, 2 * 1536], BF16)

    pb = [nc.alloc_psum_tensor(f"pb{i}", [128, 512], F32) for i in range(8)]

    Rv = R_h.ap().rearrange("p (c s) -> p c s", c=DC)
    xspv = [h.ap().rearrange("p (c t) -> p c t", c=DC) for h in xsp_h]
    hspv = hsp_h.ap().rearrange("p (c s t) -> p c s t", c=HC, s=3)
    wgslab = wslab_g_h.ap().rearrange("p (b c j) -> p b c j", b=2, c=DC)
    wuslab = wslab_u_h.ap().rearrange("p (b c j) -> p b c j", b=2, c=DC)
    wdslab = wslab_d_h.ap().rearrange("p (b c j) -> p b c j", b=3, c=8)
    iov = iostage.ap().rearrange("p (b q) -> p b q", b=2)
    xn2v = xn2.ap().rearrange("p (b t) -> p b t", b=2)
    xsqv = xsq.ap().rearrange("p (b t) -> p b t", b=2)
    axsqv = axsq.ap().rearrange("p (b t) -> p b t", b=2)
    awv = {nm: aw[nm].ap().rearrange("p (c n) -> p c n", c=DC) for nm in aw}
    awp_v = aw_p.ap().rearrange("p (c j) -> p c j", c=DC)
    rows = rows_h.ap()
    r_mu = rows[:, 0 * T:1 * T]
    r_t = rows[:, 1 * T:2 * T]
    r_A = rows[:, 2 * T:3 * T]
    r_b = rows[:, 3 * T:4 * T]

    import contextlib
    stack = contextlib.ExitStack()
    SW = stack.enter_context(nc.semaphore("dmaw"))
    SIO = stack.enter_context(nc.semaphore("dmaio"))
    PE = stack.enter_context(nc.semaphore("pe"))
    VE = stack.enter_context(nc.semaphore("ve"))
    SE = stack.enter_context(nc.semaphore("se"))
    GE = stack.enter_context(nc.semaphore("ge"))
    DOUT = stack.enter_context(nc.semaphore("dout"))

    sch = Sched()
    do, wait = sch.do, sch.wait

    bank_free = {i: None for i in range(8)}
    io_free = [None, None]
    state = {"sqrt_read": None, "prev_hm": None}

    with stack, nc.Block() as block:
        t_ones1 = do("vector", lambda e: e.memset(ones_col.ap()[:], 1.0), VE)
        t_c127 = do("vector", lambda e: e.memset(c127.ap()[:], 1.0 / 127.0), VE)
        t_ones2 = do("vector", lambda e: e.memset(ones_stat.ap()[:], 1.0), VE)
        t_iota = do("gpsimd", lambda e: e.iota(ident.ap()[:], [[1, 128]], base=0,
                                               channel_multiplier=-1,
                                               allow_small_or_imprecise_dtypes=True), GE)
        wait("vector", t_iota)
        t_ident = do("vector", lambda e: e.tensor_single_scalar(
            ident.ap()[:], ident.ap()[:], 0.0, ALU.is_equal), VE)
        for dst, src in ((sb_b["i"], d_bi), (sb_b["f"], d_bf), (sb_b["z"], d_bz), (sb_b["o"], d_bo),
                         (sb_bp, d_bp), (sb_lnw, d_lnw), (sb_lnb, d_lnb)):
            sm_tok = do("sync", lambda e, dst=dst, src=src: e.dma_start(
                out=dst.ap()[:], in_=src[:]), SIO, dma=True)

        cvi = cv_i32.ap().rearrange("p (b q) -> p b q", b=2)
        cvb = cv_bf.ap().rearrange("p (b q) -> p b q", b=2)
        units = []
        for m in range(HC):
            units.append((wg_t[m], wgb[m], 1024))
            units.append((wu_t[m], wub[m], 1024))
        for m in range(DC):
            units.append((wd_t[m][:, 0:1536], wdb[m][:, 0:1536], 1536))
            units.append((wd_t[m][:, 1536:3072], wdb[m][:, 1536:3072], 1536))
        cast_tok = [None, None]
        out_tok = [None, None]
        for i, (src, dst, q) in enumerate(units):
            p = i % 2
            wait("sync", cast_tok[p])
            in_tok = do("sync", lambda e, src=src, p=p, q=q: e.dma_start(
                out=cvi[:, p, 0:q], in_=src), SW, dma=True)
            wait("vector", in_tok)
            wait("vector", out_tok[p])
            cast_tok[p] = do("vector", lambda e, p=p, q=q: e.tensor_copy(
                cvb[:, p, 0:q], cvi[:, p, 0:q]), VE)
            wait("sync", cast_tok[p])
            out_tok[p] = do("sync", lambda e, dst=dst, p=p, q=q: e.dma_start(
                out=dst, in_=cvb[:, p, 0:q]), SW, dma=True)
        conv_toks = [out_tok[0], out_tok[1]]

        def pe_writes_bank(i):
            wait("tensor", bank_free[i])

        def dma_w(dst_ap, src_ap, pre_tok):
            wait("sync", pre_tok)
            return do("sync", lambda e: e.dma_start(out=dst_ap, in_=src_ap), SW, dma=True)

        def phase_in(b):
            for sub in range(NSUB):
                p = sub % 2
                t0 = sub * 128
                wait("sync", io_free[p])
                ld = do("sync", lambda e, b=b, t0=t0, p=p: e.dma_start(
                    out=iov[:, p, :], in_=x_in[b, t0:t0 + 128, :]), SIO, dma=True)
                wait("tensor", ld)
                wait("tensor", t_ident)
                last_ev = None
                for half in range(2):
                    bi = 6 + half
                    pe_writes_bank(bi)
                    tr = None
                    for c4 in range(4):
                        c = half * 4 + c4
                        tr = do("tensor", lambda e, p=p, c=c, c4=c4, bi=bi: e.transpose(
                            pb[bi].ap()[:, c4 * 128:(c4 + 1) * 128],
                            iov[:, p, c * 128:(c + 1) * 128], ident.ap()[:]),
                            PE if c4 == 3 else None)
                    wait("vector", tr)
                    ev = do("vector", lambda e, t0=t0, half=half, bi=bi: e.tensor_copy(
                        Rv[:, half * 4:(half + 1) * 4, t0:t0 + 128],
                        pb[bi].ap().rearrange("p (c t) -> p c t", c=4)[:]), VE)
                    bank_free[bi] = ev
                    last_ev = ev
                io_free[p] = last_ev
            return last_ev

        def ln_tile(b, tt, kind, r_ready_tok, sq, ar, br):
            t0 = tt * T
            if r_ready_tok is not None:
                wait("scalar", r_ready_tok)
                wait("tensor", r_ready_tok)
            wait("vector", state["sqrt_read"])
            wait("tensor", t_ones2)
            sq_pe = [None, None]
            mm_sum = mm_sq = None
            for c in range(DC):
                pp = c % 2
                wait("scalar", sq_pe[pp])
                sqt = do("scalar", lambda e, c=c, t0=t0, pp=pp, sq=sq: e.activation(
                    sq[:, pp, :], Rv[:, c, t0:t0 + T], ACT.Square), SE)
                if c < 2:
                    pe_writes_bank(c)
                mm_sum = do("tensor", lambda e, c=c, t0=t0: e.matmul(
                    pb[0].ap()[0:1, :], ones_stat.ap()[:], Rv[:, c, t0:t0 + T],
                    start=(c == 0), stop=(c == DC - 1)), PE if c == DC - 1 else None)
                wait("tensor", sqt)
                sq_pe[pp] = do("tensor", lambda e, c=c, pp=pp, sq=sq: e.matmul(
                    pb[1].ap()[0:1, :], ones_stat.ap()[:], sq[:, pp, :],
                    start=(c == 0), stop=(c == DC - 1)), PE)
                if c == DC - 1:
                    mm_sq = sq_pe[pp]
            wait("vector", mm_sum)
            do("vector", lambda e: e.tensor_scalar(r_mu, pb[0].ap()[0:1, :], 1.0 / D, None, ALU.mult), VE)
            wait("vector", mm_sq)
            st_read = do("vector", lambda e: e.tensor_scalar(r_t, pb[1].ap()[0:1, :], 1.0 / D, None, ALU.mult), VE)
            bank_free[0] = st_read
            bank_free[1] = st_read
            do("vector", lambda e: e.tensor_mul(r_b, r_mu, r_mu), VE)
            do("vector", lambda e: e.tensor_sub(r_t, r_t, r_b), VE)
            if kind == "ffn":
                do("vector", lambda e: e.tensor_scalar(r_t, r_t, EPS, None, ALU.add), VE)
                tv = do("vector", lambda e: e.reciprocal(r_t, r_t), VE)
                wait("scalar", tv)
                tA = do("scalar", lambda e: e.activation(r_A, r_t, ACT.Sqrt), SE)
                state["sqrt_read"] = tA
            else:
                do("vector", lambda e: e.tensor_scalar(r_b, r_t, EPS, None, ALU.add), VE)
                tv = do("vector", lambda e: e.reciprocal(r_b, r_b), VE)
                wait("scalar", tv)
                tA1 = do("scalar", lambda e: e.activation(r_A, r_b, ACT.Sqrt), SE)
                wait("vector", tA1)
                do("vector", lambda e: e.tensor_mul(r_t, r_t, r_b), VE)
                do("vector", lambda e: e.tensor_scalar(r_t, r_t, EPS, None, ALU.add), VE)
                tv2 = do("vector", lambda e: e.reciprocal(r_t, r_t), VE)
                wait("scalar", tv2)
                tA2 = do("scalar", lambda e: e.activation(r_t, r_t, ACT.Sqrt), SE)
                state["sqrt_read"] = tA2
                wait("vector", tA2)
                tA = do("vector", lambda e: e.tensor_mul(r_A, r_A, r_t), VE)
            wait("vector", tA)
            do("vector", lambda e: e.tensor_scalar(r_b, r_mu, -1.0, None, ALU.mult), VE)
            tB = do("vector", lambda e: e.tensor_mul(r_b, r_b, r_A), VE)
            wait("tensor", tB)
            wait("tensor", t_ones1)
            pe_writes_bank(2)
            mA = do("tensor", lambda e: e.matmul(pb[2].ap()[:], ones_col.ap()[:], r_A,
                                                 start=True, stop=True), PE)
            pe_writes_bank(3)
            mB = do("tensor", lambda e: e.matmul(pb[3].ap()[:], ones_col.ap()[:], r_b,
                                                 start=True, stop=True), PE)
            wait("vector", mA)
            tAr = do("vector", lambda e, ar=ar: e.tensor_copy(ar[:], pb[2].ap()[:]), VE)
            bank_free[2] = tAr
            wait("vector", mB)
            tBr = do("vector", lambda e, br=br: e.tensor_copy(br[:], pb[3].ap()[:]), VE)
            bank_free[3] = tBr
            return tAr, tBr

        def split_chunk(b, tt, c, tAr, tBr, pe_done_with_xsp):
            t0 = tt * T
            wait("vector", tAr)
            wait("vector", tBr)
            wait("vector", pe_done_with_xsp)
            do("vector", lambda e, c=c, t0=t0: e.tensor_mul(
                xn_tmp.ap()[:], Rv[:, c, t0:t0 + T], arep.ap()[:]), VE)
            do("vector", lambda e: e.tensor_add(xn_tmp.ap()[:], xn_tmp.ap()[:], brep.ap()[:]), VE)
            do("vector", lambda e, c=c: e.tensor_copy(xspv[0][:, c, :], xn_tmp.ap()[:]), VE)
            do("vector", lambda e, c=c: e.tensor_sub(xn_tmp.ap()[:], xn_tmp.ap()[:], xspv[0][:, c, :]), VE)
            do("vector", lambda e, c=c: e.tensor_copy(xspv[1][:, c, :], xn_tmp.ap()[:]), VE)
            do("vector", lambda e, c=c: e.tensor_sub(xn_tmp.ap()[:], xn_tmp.ap()[:], xspv[1][:, c, :]), VE)
            return do("vector", lambda e, c=c: e.tensor_copy(xspv[2][:, c, :], xn_tmp.ap()[:]), VE)

        slab_state = {"g_tok": [None, None], "g_free": [None, None],
                      "u_tok": [None, None], "u_free": [None, None],
                      "d_tok": [None, None, None], "d_free": [None, None, None]}

        def ffn_sublayer(b, first_r_tok, conv_pre):
            gs, gf = slab_state["g_tok"], slab_state["g_free"]
            us, uf = slab_state["u_tok"], slab_state["u_free"]
            ds, df = slab_state["d_tok"], slab_state["d_free"]
            split_done = None
            pending_ln = None
            r_written = None
            for tt in range(NT):
                t0 = tt * T
                if tt == 0 or not INTERLEAVE:
                    tAr, tBr = ln_tile(b, tt, "ffn", first_r_tok if tt == 0 else None,
                                       xsqv, arep.ap(), brep.ap())
                    for c in range(DC):
                        split_done = split_chunk(b, tt, c, tAr, tBr, state.get("xsp_free"))
                hsp_ready = None
                for ht in range(HC):
                    pp = ht % 2
                    gs[pp] = dma_w(wgslab[:, pp, :, :],
                                   wgb[ht].rearrange("p (c j) -> p c j", c=DC)[:],
                                   gf[pp] if gf[pp] is not None else conv_pre)
                    us[pp] = dma_w(wuslab[:, pp, :, :],
                                   wub[ht].rearrange("p (c j) -> p c j", c=DC)[:],
                                   uf[pp] if uf[pp] is not None else conv_pre)
                    wait("tensor", gs[pp])
                    wait("tensor", split_done)
                    pe_writes_bank(pp)
                    mm = None
                    for c in range(DC):
                        for s_i in range(3):
                            last = (c == DC - 1 and s_i == 2)
                            mm = do("tensor", lambda e, pp=pp, c=c, s_i=s_i: e.matmul(
                                pb[pp].ap()[:], wgslab[:, pp, c, :], xspv[s_i][:, c, :],
                                start=(c == 0 and s_i == 0), stop=last),
                                PE if last else None)
                    gate_mm = mm
                    gf[pp] = gate_mm
                    wait("tensor", us[pp])
                    pe_writes_bank(2 + pp)
                    for c in range(DC):
                        for s_i in range(3):
                            last = (c == DC - 1 and s_i == 2)
                            mm = do("tensor", lambda e, pp=pp, c=c, s_i=s_i: e.matmul(
                                pb[2 + pp].ap()[:], wuslab[:, pp, c, :], xspv[s_i][:, c, :],
                                start=(c == 0 and s_i == 0), stop=last),
                                PE if last else None)
                    up_mm = mm
                    uf[pp] = up_mm
                    if ht == HC - 1:
                        state["xsp_free"] = up_mm
                    wait("scalar", gate_mm)
                    wait("scalar", state["prev_hm"])
                    wait("scalar", t_c127)
                    sil = do("scalar", lambda e, pp=pp: e.activation(
                        g_tmp.ap()[:], pb[pp].ap()[:], ACT.Silu, scale=c127.ap()[:, 0:1]), SE)
                    bank_free[pp] = sil
                    wait("vector", sil)
                    wait("vector", up_mm)
                    hm = do("vector", lambda e, pp=pp: e.scalar_tensor_tensor(
                        h_tmp.ap()[:], pb[2 + pp].ap()[:], 1.0 / 127.0, g_tmp.ap()[:],
                        ALU.mult, ALU.mult), VE)
                    bank_free[2 + pp] = hm
                    state["prev_hm"] = hm
                    do("vector", lambda e, ht=ht: e.tensor_copy(hspv[:, ht, 0, :], h_tmp.ap()[:]), VE)
                    do("vector", lambda e, ht=ht: e.tensor_sub(h_tmp.ap()[:], h_tmp.ap()[:], hspv[:, ht, 0, :]), VE)
                    do("vector", lambda e, ht=ht: e.tensor_copy(hspv[:, ht, 1, :], h_tmp.ap()[:]), VE)
                    do("vector", lambda e, ht=ht: e.tensor_sub(h_tmp.ap()[:], h_tmp.ap()[:], hspv[:, ht, 1, :]), VE)
                    hsp_ready = do("vector", lambda e, ht=ht: e.tensor_copy(hspv[:, ht, 2, :], h_tmp.ap()[:]), VE)
                next_ln = None
                for dt in range(DC):
                    for third in range(3):
                        ds[third] = dma_w(
                            wdslab[:, third, :, :],
                            wdb[dt].rearrange("p (c j) -> p c j", c=HC)[:, third * 8:(third + 1) * 8, :],
                            df[third] if df[third] is not None else conv_pre)
                    ppd = dt % 2
                    wait("tensor", hsp_ready)
                    pe_writes_bank(4 + ppd)
                    for third in range(3):
                        wait("tensor", ds[third])
                        mm = None
                        for cl in range(8):
                            c = third * 8 + cl
                            for s_i in range(3):
                                last_third = (cl == 7 and s_i == 2)
                                last_all = (c == HC - 1 and s_i == 2)
                                mm = do("tensor", lambda e, third=third, cl=cl, c=c, s_i=s_i, ppd=ppd, last_all=last_all: e.matmul(
                                    pb[4 + ppd].ap()[:], wdslab[:, third, cl, :], hspv[:, c, s_i, :],
                                    start=(c == 0 and s_i == 0), stop=last_all),
                                    PE if last_third else None)
                        df[third] = mm
                    down_mm = mm
                    wait("vector", down_mm)
                    r_written = do("vector", lambda e, dt=dt, t0=t0, ppd=ppd: e.scalar_tensor_tensor(
                        Rv[:, dt, t0:t0 + T], pb[4 + ppd].ap()[:], 0.5 / 127.0, Rv[:, dt, t0:t0 + T],
                        ALU.mult, ALU.add), VE)
                    bank_free[4 + ppd] = r_written
                    if INTERLEAVE and tt + 1 < NT:
                        if dt == 0:
                            next_ln = ln_tile(b, tt + 1, "ffn", None, xsqv, arep.ap(), brep.ap())
                        tAr2, tBr2 = next_ln
                        split_done = split_chunk(b, tt + 1, dt, tAr2, tBr2,
                                                 state.get("xsp_free"))
            return r_written

        def attn_sublayer(b, r_tok):
            wait("tensor", sm_tok)
            wait("vector", sm_tok)
            wait("sync", r_tok)
            awtok = None
            for nm, src in (("i", wi_t), ("f", wf_t), ("z", wz_t), ("o", wo_t)):
                awtok = do("sync", lambda e, nm=nm, src=src: e.dma_start(
                    out=awv[nm][:], in_=src.rearrange("c p n -> p c n")[:]), SIO, dma=True)
            awtok = do("sync", lambda e: e.dma_start(
                out=awp_v[:], in_=wp_t.rearrange("c p j -> p c j")[:]), SIO, dma=True)
            wait("tensor", awtok)
            prev_evac = [None, None, None, None]
            stats_read = None
            xn_pe = [None, None]
            for tt in range(NT):
                t0 = tt * T
                tAr, tBr = ln_tile(b, tt, "attn", r_tok if tt == 0 else None,
                                   axsqv, aarep.ap(), abrep.ap())
                mm_tok = [None, None, None, None]
                for c in range(DC):
                    pp = c % 2
                    wait("vector", tAr)
                    wait("vector", tBr)
                    wait("vector", xn_pe[pp])
                    do("vector", lambda e, c=c, t0=t0, pp=pp: e.tensor_mul(
                        xn2v[:, pp, :], Rv[:, c, t0:t0 + T], aarep.ap()[:]), VE)
                    do("vector", lambda e, pp=pp: e.tensor_add(xn2v[:, pp, :], xn2v[:, pp, :], abrep.ap()[:]), VE)
                    xt = do("vector", lambda e, c=c, pp=pp: e.tensor_scalar(
                        xn2v[:, pp, :], xn2v[:, pp, :], sb_lnw.ap()[:, c:c + 1], sb_lnb.ap()[:, c:c + 1],
                        ALU.mult, ALU.add), VE)
                    wait("tensor", xt)
                    for j, nm in enumerate(("i", "f", "z", "o")):
                        if c == 0:
                            pe_writes_bank(j)
                        last = (c == DC - 1)
                        tok = do("tensor", lambda e, nm=nm, j=j, c=c, pp=pp: e.matmul(
                            pb[j].ap()[0:NL, :], awv[nm][:, c, :], xn2v[:, pp, :],
                            start=(c == 0), stop=last),
                            PE if (last or nm == "o") else None)
                        if nm == "o":
                            xn_pe[pp] = tok
                        if last:
                            mm_tok[j] = tok
                wait("vector", mm_tok[0])
                do("vector", lambda e, t0=t0: e.tensor_scalar(
                    sc_li.ap()[:, t0:t0 + T], pb[0].ap()[0:NL, :], sb_b["i"].ap()[:, 0:1], -20.0,
                    ALU.add, ALU.max), VE)
                prev_evac[0] = do("vector", lambda e, t0=t0: e.tensor_scalar(
                    sc_li.ap()[:, t0:t0 + T], sc_li.ap()[:, t0:t0 + T], 20.0, None, ALU.min), VE)
                bank_free[0] = prev_evac[0]
                wait("vector", mm_tok[1])
                do("vector", lambda e, t0=t0: e.tensor_scalar(
                    sc_lf.ap()[:, t0:t0 + T], pb[1].ap()[0:NL, :], sb_b["f"].ap()[:, 0:1], -20.0,
                    ALU.add, ALU.max), VE)
                prev_evac[1] = do("vector", lambda e, t0=t0: e.tensor_scalar(
                    sc_lf.ap()[:, t0:t0 + T], sc_lf.ap()[:, t0:t0 + T], 20.0, None, ALU.min), VE)
                bank_free[1] = prev_evac[1]
                wait("vector", mm_tok[2])
                prev_evac[2] = do("vector", lambda e, t0=t0: e.tensor_scalar(
                    sc_z.ap()[:, t0:t0 + T], pb[2].ap()[0:NL, :], sb_b["z"].ap()[:, 0:1], None, ALU.add), VE)
                bank_free[2] = prev_evac[2]
                wait("vector", mm_tok[3])
                prev_evac[3] = do("vector", lambda e, t0=t0: e.tensor_scalar(
                    sc_o.ap()[:, t0:t0 + T], pb[3].ap()[0:NL, :], sb_b["o"].ap()[:, 0:1], None, ALU.add), VE)
                bank_free[3] = prev_evac[3]
            wait("scalar", prev_evac[2])
            t_z = do("scalar", lambda e: e.activation(sc_z.ap()[:], sc_z.ap()[:], ACT.Tanh), SE)
            do("vector", lambda e: e.memset(sc_m.ap()[:, 0:1], 0.0), VE)
            wait("vector", prev_evac[0])
            wait("vector", prev_evac[1])
            t_m = do("vector", lambda e: e.tensor_tensor_scan(
                sc_m.ap()[:, 1:S + 1], sc_lf.ap()[:], sc_li.ap()[:], 0.0, ALU.add, ALU.max), VE)
            do("vector", lambda e: e.tensor_sub(sc_lf.ap()[:], sc_lf.ap()[:], sc_m.ap()[:, 1:S + 1]), VE)
            t_a = do("vector", lambda e: e.tensor_add(sc_lf.ap()[:], sc_lf.ap()[:], sc_m.ap()[:, 0:S]), VE)
            wait("scalar", t_a)
            t_fs = do("scalar", lambda e: e.activation(sc_lf.ap()[:], sc_lf.ap()[:], ACT.Exp), SE)
            t_bv = do("vector", lambda e: e.tensor_sub(sc_li.ap()[:], sc_li.ap()[:], sc_m.ap()[:, 1:S + 1]), VE)
            wait("scalar", t_bv)
            t_is = do("scalar", lambda e: e.activation(sc_li.ap()[:], sc_li.ap()[:], ACT.Exp), SE)
            wait("vector", t_is)
            t_iz = do("vector", lambda e: e.tensor_mul(sc_z.ap()[:], sc_li.ap()[:], sc_z.ap()[:]), VE)
            wait("vector", t_fs)
            t_c = do("vector", lambda e: e.tensor_tensor_scan(
                sc_m.ap()[:, 0:S], sc_lf.ap()[:], sc_z.ap()[:], 0.0, ALU.mult, ALU.add), VE)
            wait("scalar", t_c)
            t_tc = do("scalar", lambda e: e.activation(sc_m.ap()[:, 0:S], sc_m.ap()[:, 0:S], ACT.Tanh), SE)
            wait("scalar", prev_evac[3])
            t_o = do("scalar", lambda e: e.activation(sc_o.ap()[:], sc_o.ap()[:], ACT.Sigmoid), SE)
            wait("vector", t_o)
            t_h = do("vector", lambda e: e.tensor_mul(sc_o.ap()[:], sc_o.ap()[:], sc_m.ap()[:, 0:S]), VE)
            r_written = None
            wait("tensor", t_h)
            for tt in range(NT):
                t0 = tt * T
                for dt in range(DC):
                    ppd = (tt * DC + dt) % 2
                    pe_writes_bank(4 + ppd)
                    mo = do("tensor", lambda e, dt=dt, t0=t0, ppd=ppd: e.matmul(
                        pb[4 + ppd].ap()[:], awp_v[:, dt, :], sc_o.ap()[:, t0:t0 + T],
                        start=True, stop=True), PE)
                    wait("vector", mo)
                    r_written = do("vector", lambda e, dt=dt, t0=t0, ppd=ppd: e.scalar_tensor_tensor(
                        Rv[:, dt, t0:t0 + T], pb[4 + ppd].ap()[:], sb_bp.ap()[:, dt:dt + 1],
                        Rv[:, dt, t0:t0 + T], ALU.add, ALU.add), VE)
                    bank_free[4 + ppd] = r_written
            return r_written

        last_out = [None]

        def phase_out(b, r_tok):
            wait("tensor", r_tok)
            for sub in range(NSUB):
                p = sub % 2
                t0 = sub * 128
                evs = []
                for half in range(2):
                    bi = 6 + half
                    pe_writes_bank(bi)
                    tr = None
                    for c4 in range(4):
                        c = half * 4 + c4
                        tr = do("tensor", lambda e, c=c, c4=c4, t0=t0, bi=bi: e.transpose(
                            pb[bi].ap()[:, c4 * 128:(c4 + 1) * 128],
                            Rv[:, c, t0:t0 + 128], ident.ap()[:]),
                            PE if c4 == 3 else None)
                    wait("vector", tr)
                    wait("vector", io_free[p])
                    ev = do("vector", lambda e, p=p, half=half, bi=bi: e.tensor_copy(
                        iov[:, p, half * 512:(half + 1) * 512], pb[bi].ap()[:]), VE)
                    bank_free[bi] = ev
                    evs.append(ev)
                wait("sync", evs[0])
                wait("sync", evs[1])
                st = do("sync", lambda e, b=b, t0=t0, p=p: e.dma_start(
                    out=y_out[b, t0:t0 + 128, :], in_=iov[:, p, :]), DOUT, dma=True)
                io_free[p] = st
            last_out[0] = st

        conv_pre = conv_toks[1]
        wait("sync", conv_toks[0])
        wait("sync", conv_toks[1])
        for b in range(n_batches):
            r_tok = phase_in(b)
            state["xsp_free"] = None
            r_tok = ffn_sublayer(b, r_tok, conv_pre)
            r_tok = attn_sublayer(b, r_tok)
            state["xsp_free"] = None
            r_tok = ffn_sublayer(b, r_tok, conv_pre)
            phase_out(b, r_tok)
        do("sync", lambda e: e.wait_ge(DOUT, sch.semv[DOUT]))
        sch.emit(block)
    return nc




def kernel(**inputs):
    x = np.ascontiguousarray(np.asarray(inputs["L0_attn_x"], dtype=np.float32))
    n_cores = 8
    prep = host_prepare(inputs)
    nc = build_program(n_batches=B_CORE)
    in_maps = []
    for c in range(n_cores):
        m = {"x_in": np.ascontiguousarray(x[c * B_CORE:(c + 1) * B_CORE])}
        m.update(prep)
        in_maps.append(m)
    from concourse.bass_utils import run_bass_kernel_spmd
    res = run_bass_kernel_spmd(nc, in_maps, core_ids=list(range(n_cores)))
    return np.concatenate([r["y_out"] for r in res.results], axis=0)


# version 03
# speedup vs baseline: 1.0522x; 1.0522x over previous
import numpy as np
import ml_dtypes
import concourse.bass as bass
import concourse.mybir as mybir
from concourse.bass_utils import run_bass_kernel_spmd

F32 = mybir.dt.float32
BF16 = mybir.dt.bfloat16
I32 = mybir.dt.int32
ALU = mybir.AluOpType
ACT = mybir.ActivationFunctionType

INTERLEAVE = True
B_CORE = 2
S = 2048
D = 1024
DC = 8
H = 3072
HC = 24
NL = 64
T = 512
NT = S // T
NSUB = S // 128
EPS = 1e-5
ENGS = ("sync", "tensor", "vector", "scalar", "gpsimd")


class Sched:
    def __init__(self):
        self.ops = {e: [] for e in ENGS}
        self.semv = {}
        self.last_wait = {}

    def do(self, eng, fn, sem=None, inc=1, dma=False):
        if sem is None:
            self.ops[eng].append(fn)
            return None
        if dma:
            inc = 16
        self.ops[eng].append(lambda e, fn=fn, sem=sem, inc=inc: fn(e).then_inc(sem, inc))
        v = self.semv.get(sem, 0) + inc
        self.semv[sem] = v
        return (sem, v, "dma" if dma else eng)

    def wait(self, eng, tok):
        if tok is None:
            return
        sem, v, prod = tok
        if prod == eng:
            return
        key = (eng, sem)
        if self.last_wait.get(key, -1) >= v:
            return
        self.last_wait[key] = v
        self.ops[eng].append(lambda e, sem=sem, v=v: e.wait_ge(sem, v))

    def emit(self, block):
        for eng in ENGS:
            ops = self.ops[eng]
            def body(e, ops=ops):
                for fn in ops:
                    fn(e)
            getattr(block, eng)(body)


def host_prepare(inputs):
    out = {}
    wg = np.asarray(inputs["L0_ffn_w_gate"])
    wu = np.asarray(inputs["L0_ffn_w_up"])
    wd = np.asarray(inputs["L0_ffn_w_down"])
    out["wg_t"] = np.ascontiguousarray(wg.reshape(HC, 128, DC, 128).transpose(0, 3, 2, 1)).reshape(HC, 128, DC * 128)
    out["wu_t"] = np.ascontiguousarray(wu.reshape(HC, 128, DC, 128).transpose(0, 3, 2, 1)).reshape(HC, 128, DC * 128)
    out["wd_t"] = np.ascontiguousarray(wd.reshape(DC, 128, HC, 128).transpose(0, 3, 2, 1)).reshape(DC, 128, HC * 128)
    for nm, w in (("wi_t", "L0_attn_Wi"), ("wf_t", "L0_attn_Wf"),
                  ("wz_t", "L0_attn_Wz"), ("wo_t", "L0_attn_Wo")):
        W = np.asarray(inputs[w])
        out[nm] = np.ascontiguousarray(W.reshape(NL, DC, 128).transpose(1, 2, 0))
    Wp = np.asarray(inputs["L0_attn_Wp"])
    out["wp_t"] = np.ascontiguousarray(Wp.reshape(DC, 128, NL).transpose(0, 2, 1))
    for nm, k in (("d_bi", "L0_attn_bi"), ("d_bf", "L0_attn_bf"),
                  ("d_bz", "L0_attn_bz"), ("d_bo", "L0_attn_bo")):
        out[nm] = np.ascontiguousarray(np.asarray(inputs[k]).reshape(NL, 1))
    for nm, k in (("d_bp", "L0_attn_bp"), ("d_lnw", "L0_attn_ln_w"), ("d_lnb", "L0_attn_ln_b")):
        out[nm] = np.ascontiguousarray(np.asarray(inputs[k]).reshape(DC, 128).T)
    return out


def build_program(n_batches=B_CORE):
    nc = bass.Bass("TRN2", target_bir_lowering=False, debug=False)

    x_in = nc.dram_tensor("x_in", [B_CORE, S, D], F32, kind="ExternalInput").ap()
    y_out = nc.dram_tensor("y_out", [B_CORE, S, D], F32, kind="ExternalOutput").ap()
    wg_t = nc.dram_tensor("wg_t", [HC, 128, DC * 128], I32, kind="ExternalInput").ap()
    wu_t = nc.dram_tensor("wu_t", [HC, 128, DC * 128], I32, kind="ExternalInput").ap()
    wd_t = nc.dram_tensor("wd_t", [DC, 128, HC * 128], I32, kind="ExternalInput").ap()
    wi_t = nc.dram_tensor("wi_t", [DC, 128, NL], F32, kind="ExternalInput").ap()
    wf_t = nc.dram_tensor("wf_t", [DC, 128, NL], F32, kind="ExternalInput").ap()
    wz_t = nc.dram_tensor("wz_t", [DC, 128, NL], F32, kind="ExternalInput").ap()
    wo_t = nc.dram_tensor("wo_t", [DC, 128, NL], F32, kind="ExternalInput").ap()
    wp_t = nc.dram_tensor("wp_t", [DC, NL, 128], F32, kind="ExternalInput").ap()
    d_bi = nc.dram_tensor("d_bi", [NL, 1], F32, kind="ExternalInput").ap()
    d_bf = nc.dram_tensor("d_bf", [NL, 1], F32, kind="ExternalInput").ap()
    d_bz = nc.dram_tensor("d_bz", [NL, 1], F32, kind="ExternalInput").ap()
    d_bo = nc.dram_tensor("d_bo", [NL, 1], F32, kind="ExternalInput").ap()
    d_bp = nc.dram_tensor("d_bp", [128, DC], F32, kind="ExternalInput").ap()
    d_lnw = nc.dram_tensor("d_lnw", [128, DC], F32, kind="ExternalInput").ap()
    d_lnb = nc.dram_tensor("d_lnb", [128, DC], F32, kind="ExternalInput").ap()
    wgb = nc.dram_tensor("wgb", [HC, 128, DC * 128], BF16, kind="Internal").ap()
    wub = nc.dram_tensor("wub", [HC, 128, DC * 128], BF16, kind="Internal").ap()
    wdb = nc.dram_tensor("wdb", [DC, 128, HC * 128], BF16, kind="Internal").ap()

    R_h = nc.alloc_sbuf_tensor("R", [128, DC * S], F32)
    wslab_g_h = nc.alloc_sbuf_tensor("wslab_g", [128, 2 * DC * 128], BF16)
    wslab_u_h = nc.alloc_sbuf_tensor("wslab_u", [128, 2 * DC * 128], BF16)
    wslab_d_h = nc.alloc_sbuf_tensor("wslab_d", [128, 3 * 8 * 128], BF16)
    sb_b = {}
    for nm in ("i", "f", "z", "o"):
        sb_b[nm] = nc.alloc_sbuf_tensor(f"sb_b{nm}", [NL, 1], F32)
    sb_bp = nc.alloc_sbuf_tensor("sb_bp", [128, DC], F32)
    sb_lnw = nc.alloc_sbuf_tensor("sb_lnw", [128, DC], F32)
    sb_lnb = nc.alloc_sbuf_tensor("sb_lnb", [128, DC], F32)
    ones_col = nc.alloc_sbuf_tensor("ones_col", [1, 128], F32)
    c127 = nc.alloc_sbuf_tensor("c127", [128, 1], F32)
    ones_stat = nc.alloc_sbuf_tensor("ones_stat", [128, 1], F32)
    ident = nc.alloc_sbuf_tensor("ident", [128, 128], F32)
    rows_h = nc.alloc_sbuf_tensor("rows", [1, 4 * T], F32)
    st_acc = nc.alloc_sbuf_tensor("st_acc", [128, T], F32)
    st_acc2 = nc.alloc_sbuf_tensor("st_acc2", [128, T], F32)

    ARENA = 112 * 1024
    base = nc.sbuf_base
    assert base + ARENA <= nc.sbuf_top - 64, (base, nc.sbuf_top)
    off = [base]

    def arena(name, shape, dtype):
        nbytes = int(np.prod(shape[1:])) * mybir.dt.size(dtype)
        t = nc.alloc_sbuf_tensor_at(name, shape, dtype, offset=off[0])
        off[0] += (nbytes + 31) & ~31
        assert off[0] <= base + ARENA, (name, off[0] - base)
        return t

    xsp_h = [arena(f"xsp{i}", [128, DC * T], BF16) for i in range(3)]
    hsp_h = arena("hsp", [128, HC * 3 * T], BF16)
    xn_tmp = arena("xn_tmp", [128, T], F32)
    xsq = arena("xsq", [128, 2 * T], F32)
    g_tmp = arena("g_tmp", [128, T], F32)
    arep = arena("arep", [128, T], F32)
    brep = arena("brep", [128, T], F32)
    h_tmp = arena("h_tmp", [128, T], F32)

    off[0] = base
    iostage = arena("iostage", [128, 2 * 1024], F32)
    off[0] = base + 8 * 1024
    sc_li = arena("sc_li", [NL, S], F32)
    sc_lf = arena("sc_lf", [NL, S], F32)
    sc_z = arena("sc_z", [NL, S], F32)
    sc_o = arena("sc_o", [NL, S], F32)
    sc_m = arena("sc_m", [NL, S + 4], F32)
    xn2 = arena("xn2", [128, 2 * T], F32)
    axsq = arena("axsq", [128, 2 * T], F32)
    aarep = arena("aarep", [128, T], F32)
    abrep = arena("abrep", [128, T], F32)
    aw = {}
    for nm in ("i", "f", "z", "o"):
        aw[nm] = arena(f"aw_{nm}", [128, DC * NL], F32)
    aw_p = arena("aw_p", [NL, DC * 128], F32)

    pb = [nc.alloc_psum_tensor(f"pb{i}", [128, 512], F32) for i in range(8)]

    Rv = R_h.ap().rearrange("p (c s) -> p c s", c=DC)
    xspv = [h.ap().rearrange("p (c t) -> p c t", c=DC) for h in xsp_h]
    hspv = hsp_h.ap().rearrange("p (c s t) -> p c s t", c=HC, s=3)
    wgslab = wslab_g_h.ap().rearrange("p (b c j) -> p b c j", b=2, c=DC)
    wuslab = wslab_u_h.ap().rearrange("p (b c j) -> p b c j", b=2, c=DC)
    wdslab = wslab_d_h.ap().rearrange("p (b c j) -> p b c j", b=3, c=8)
    iov = iostage.ap().rearrange("p (b q) -> p b q", b=2)
    xn2v = xn2.ap().rearrange("p (b t) -> p b t", b=2)
    xsqv = xsq.ap().rearrange("p (b t) -> p b t", b=2)
    axsqv = axsq.ap().rearrange("p (b t) -> p b t", b=2)
    awv = {nm: aw[nm].ap().rearrange("p (c n) -> p c n", c=DC) for nm in aw}
    awp_v = aw_p.ap().rearrange("p (c j) -> p c j", c=DC)
    rows = rows_h.ap()
    r_mu = rows[:, 0 * T:1 * T]
    r_t = rows[:, 1 * T:2 * T]
    r_A = rows[:, 2 * T:3 * T]
    r_b = rows[:, 3 * T:4 * T]

    import contextlib
    stack = contextlib.ExitStack()
    SW = stack.enter_context(nc.semaphore("dmaw"))
    SIO = stack.enter_context(nc.semaphore("dmaio"))
    PE = stack.enter_context(nc.semaphore("pe"))
    VE = stack.enter_context(nc.semaphore("ve"))
    SE = stack.enter_context(nc.semaphore("se"))
    GE = stack.enter_context(nc.semaphore("ge"))
    DOUT = stack.enter_context(nc.semaphore("dout"))
    SCV = stack.enter_context(nc.semaphore("scv"))

    sch = Sched()
    do, wait = sch.do, sch.wait

    bank_free = {i: None for i in range(8)}
    io_free = [None, None]
    state = {"sqrt_read": None, "prev_hm": None}

    with stack, nc.Block() as block:
        t_ones1 = do("vector", lambda e: e.memset(ones_col.ap()[:], 1.0), VE)
        t_c127 = do("vector", lambda e: e.memset(c127.ap()[:], 1.0 / 127.0), VE)
        t_ones2 = do("vector", lambda e: e.memset(ones_stat.ap()[:], 1.0), VE)
        t_iota = do("gpsimd", lambda e: e.iota(ident.ap()[:], [[1, 128]], base=0,
                                               channel_multiplier=-1,
                                               allow_small_or_imprecise_dtypes=True), GE)
        wait("vector", t_iota)
        t_ident = do("vector", lambda e: e.tensor_single_scalar(
            ident.ap()[:], ident.ap()[:], 0.0, ALU.is_equal), VE)
        for dst, src in ((sb_b["i"], d_bi), (sb_b["f"], d_bf), (sb_b["z"], d_bz), (sb_b["o"], d_bo),
                         (sb_bp, d_bp), (sb_lnw, d_lnw), (sb_lnb, d_lnb)):
            sm_tok = do("sync", lambda e, dst=dst, src=src: e.dma_start(
                out=dst.ap()[:], in_=src[:]), SIO, dma=True)

        conv_ready = {}
        for m in range(HC):
            conv_ready[("g", m)] = do("gpsimd", lambda e, m=m: e.dma_start(out=wgb[m], in_=wg_t[m]), SCV, dma=True)
            conv_ready[("u", m)] = do("gpsimd", lambda e, m=m: e.dma_start(out=wub[m], in_=wu_t[m]), SCV, dma=True)
            if m < DC:
                conv_ready[("d", m)] = do("gpsimd", lambda e, m=m: e.dma_start(out=wdb[m], in_=wd_t[m]), SCV, dma=True)

        def pe_writes_bank(i):
            wait("tensor", bank_free[i])

        def dma_w(dst_ap, src_ap, pre_tok):
            wait("sync", pre_tok)
            return do("sync", lambda e: e.dma_start(out=dst_ap, in_=src_ap), SW, dma=True)

        def phase_in(b):
            for sub in range(NSUB):
                p = sub % 2
                t0 = sub * 128
                wait("sync", io_free[p])
                ld = do("sync", lambda e, b=b, t0=t0, p=p: e.dma_start(
                    out=iov[:, p, :], in_=x_in[b, t0:t0 + 128, :]), SIO, dma=True)
                wait("tensor", ld)
                wait("tensor", t_ident)
                last_ev = None
                for half in range(2):
                    bi = 6 + half
                    pe_writes_bank(bi)
                    tr = None
                    for c4 in range(4):
                        c = half * 4 + c4
                        tr = do("tensor", lambda e, p=p, c=c, c4=c4, bi=bi: e.transpose(
                            pb[bi].ap()[:, c4 * 128:(c4 + 1) * 128],
                            iov[:, p, c * 128:(c + 1) * 128], ident.ap()[:]),
                            PE if c4 == 3 else None)
                    wait("vector", tr)
                    ev = do("vector", lambda e, t0=t0, half=half, bi=bi: e.tensor_copy(
                        Rv[:, half * 4:(half + 1) * 4, t0:t0 + 128],
                        pb[bi].ap().rearrange("p (c t) -> p c t", c=4)[:]), VE)
                    bank_free[bi] = ev
                    last_ev = ev
                io_free[p] = last_ev
            return last_ev

        def ln_tile(b, tt, kind, r_ready_tok, sq, ar, br):
            t0 = tt * T
            if r_ready_tok is not None:
                wait("scalar", r_ready_tok)
                wait("tensor", r_ready_tok)
            wait("vector", state["sqrt_read"])
            wait("tensor", t_ones2)
            sq_pe = [None, None]
            mm_sum = mm_sq = None
            for c in range(DC):
                pp = c % 2
                wait("scalar", sq_pe[pp])
                sqt = do("scalar", lambda e, c=c, t0=t0, pp=pp, sq=sq: e.activation(
                    sq[:, pp, :], Rv[:, c, t0:t0 + T], ACT.Square), SE)
                if c < 2:
                    pe_writes_bank(c)
                mm_sum = do("tensor", lambda e, c=c, t0=t0: e.matmul(
                    pb[0].ap()[0:1, :], ones_stat.ap()[:], Rv[:, c, t0:t0 + T],
                    start=(c == 0), stop=(c == DC - 1)), PE if c == DC - 1 else None)
                wait("tensor", sqt)
                sq_pe[pp] = do("tensor", lambda e, c=c, pp=pp, sq=sq: e.matmul(
                    pb[1].ap()[0:1, :], ones_stat.ap()[:], sq[:, pp, :],
                    start=(c == 0), stop=(c == DC - 1)), PE)
                if c == DC - 1:
                    mm_sq = sq_pe[pp]
            wait("vector", mm_sum)
            do("vector", lambda e: e.tensor_scalar(r_mu, pb[0].ap()[0:1, :], 1.0 / D, None, ALU.mult), VE)
            wait("vector", mm_sq)
            st_read = do("vector", lambda e: e.tensor_scalar(r_t, pb[1].ap()[0:1, :], 1.0 / D, None, ALU.mult), VE)
            bank_free[0] = st_read
            bank_free[1] = st_read
            do("vector", lambda e: e.tensor_mul(r_b, r_mu, r_mu), VE)
            do("vector", lambda e: e.tensor_sub(r_t, r_t, r_b), VE)
            if kind == "ffn":
                do("vector", lambda e: e.tensor_scalar(r_t, r_t, EPS, None, ALU.add), VE)
                tv = do("vector", lambda e: e.reciprocal(r_t, r_t), VE)
                wait("scalar", tv)
                tA = do("scalar", lambda e: e.activation(r_A, r_t, ACT.Sqrt), SE)
            else:
                do("vector", lambda e: e.tensor_scalar(r_b, r_t, EPS, None, ALU.add), VE)
                tv = do("vector", lambda e: e.reciprocal(r_b, r_b), VE)
                wait("scalar", tv)
                tA1 = do("scalar", lambda e: e.activation(r_A, r_b, ACT.Sqrt), SE)
                wait("vector", tA1)
                do("vector", lambda e: e.tensor_mul(r_t, r_t, r_b), VE)
                do("vector", lambda e: e.tensor_scalar(r_t, r_t, EPS, None, ALU.add), VE)
                tv2 = do("vector", lambda e: e.reciprocal(r_t, r_t), VE)
                wait("scalar", tv2)
                tA2 = do("scalar", lambda e: e.activation(r_t, r_t, ACT.Sqrt), SE)
                wait("vector", tA2)
                tA = do("vector", lambda e: e.tensor_mul(r_A, r_A, r_t), VE)
            wait("vector", tA)
            do("vector", lambda e: e.tensor_scalar(r_b, r_mu, -1.0, None, ALU.mult), VE)
            tB = do("vector", lambda e: e.tensor_mul(r_b, r_b, r_A), VE)
            state["rows_busy"] = tB

            def emit_repl():
                wait("tensor", tB)
                wait("tensor", t_ones1)
                pe_writes_bank(2)
                mA = do("tensor", lambda e: e.matmul(pb[2].ap()[:], ones_col.ap()[:], r_A,
                                                     start=True, stop=True), PE)
                pe_writes_bank(3)
                mB = do("tensor", lambda e: e.matmul(pb[3].ap()[:], ones_col.ap()[:], r_b,
                                                     start=True, stop=True), PE)
                wait("vector", mA)
                tAr = do("vector", lambda e, ar=ar: e.tensor_copy(ar[:], pb[2].ap()[:]), VE)
                bank_free[2] = tAr
                wait("vector", mB)
                tBr = do("vector", lambda e, br=br: e.tensor_copy(br[:], pb[3].ap()[:]), VE)
                bank_free[3] = tBr
                state["sqrt_read"] = tBr
                return tAr, tBr
            return emit_repl

        def split_chunk(b, tt, c, tAr, tBr, pe_done_with_xsp):
            t0 = tt * T
            wait("vector", tAr)
            wait("vector", tBr)
            wait("vector", pe_done_with_xsp)
            do("vector", lambda e, c=c, t0=t0: e.tensor_mul(
                xn_tmp.ap()[:], Rv[:, c, t0:t0 + T], arep.ap()[:]), VE)
            do("vector", lambda e: e.tensor_add(xn_tmp.ap()[:], xn_tmp.ap()[:], brep.ap()[:]), VE)
            do("vector", lambda e, c=c: e.tensor_copy(xspv[0][:, c, :], xn_tmp.ap()[:]), VE)
            do("vector", lambda e, c=c: e.tensor_sub(xn_tmp.ap()[:], xn_tmp.ap()[:], xspv[0][:, c, :]), VE)
            do("vector", lambda e, c=c: e.tensor_copy(xspv[1][:, c, :], xn_tmp.ap()[:]), VE)
            do("vector", lambda e, c=c: e.tensor_sub(xn_tmp.ap()[:], xn_tmp.ap()[:], xspv[1][:, c, :]), VE)
            return do("vector", lambda e, c=c: e.tensor_copy(xspv[2][:, c, :], xn_tmp.ap()[:]), VE)

        slab_state = {"g_tok": [None, None], "g_free": [None, None],
                      "u_tok": [None, None], "u_free": [None, None],
                      "d_tok": [None, None, None], "d_free": [None, None, None]}

        def ffn_sublayer(b, first_r_tok, conv_pre):
            gs, gf = slab_state["g_tok"], slab_state["g_free"]
            us, uf = slab_state["u_tok"], slab_state["u_free"]
            ds, df = slab_state["d_tok"], slab_state["d_free"]
            split_done = None
            pending_ln = None
            r_written = None
            for tt in range(NT):
                t0 = tt * T
                if tt == 0 or not INTERLEAVE:
                    emit_repl = ln_tile(b, tt, "ffn", first_r_tok if tt == 0 else None,
                                        xsqv, arep.ap(), brep.ap())
                    tAr, tBr = emit_repl()
                    for c in range(DC):
                        split_done = split_chunk(b, tt, c, tAr, tBr, state.get("xsp_free"))
                hsp_tok = [None] * HC
                for ht in range(HC):
                    pp = ht % 2
                    wait("sync", conv_ready.get(("g", ht)))
                    gs[pp] = dma_w(wgslab[:, pp, :, :],
                                   wgb[ht].rearrange("p (c j) -> p c j", c=DC)[:],
                                   gf[pp])
                    wait("sync", conv_ready.get(("u", ht)))
                    us[pp] = dma_w(wuslab[:, pp, :, :],
                                   wub[ht].rearrange("p (c j) -> p c j", c=DC)[:],
                                   uf[pp])
                    wait("tensor", gs[pp])
                    wait("tensor", split_done)
                    pe_writes_bank(pp)
                    mm = None
                    for c in range(DC):
                        for s_i in range(3):
                            last = (c == DC - 1 and s_i == 2)
                            mm = do("tensor", lambda e, pp=pp, c=c, s_i=s_i: e.matmul(
                                pb[pp].ap()[:], wgslab[:, pp, c, :], xspv[s_i][:, c, :],
                                start=(c == 0 and s_i == 0), stop=last),
                                PE if last else None)
                    gate_mm = mm
                    gf[pp] = gate_mm
                    wait("tensor", us[pp])
                    pe_writes_bank(2 + pp)
                    for c in range(DC):
                        for s_i in range(3):
                            last = (c == DC - 1 and s_i == 2)
                            mm = do("tensor", lambda e, pp=pp, c=c, s_i=s_i: e.matmul(
                                pb[2 + pp].ap()[:], wuslab[:, pp, c, :], xspv[s_i][:, c, :],
                                start=(c == 0 and s_i == 0), stop=last),
                                PE if last else None)
                    up_mm = mm
                    uf[pp] = up_mm
                    if ht == HC - 1:
                        state["xsp_free"] = up_mm
                    wait("scalar", gate_mm)
                    wait("scalar", state["prev_hm"])
                    wait("scalar", t_c127)
                    sil = do("scalar", lambda e, pp=pp: e.activation(
                        g_tmp.ap()[:], pb[pp].ap()[:], ACT.Silu, scale=c127.ap()[:, 0:1]), SE)
                    bank_free[pp] = sil
                    wait("vector", sil)
                    wait("vector", up_mm)
                    hm = do("vector", lambda e, pp=pp: e.scalar_tensor_tensor(
                        h_tmp.ap()[:], pb[2 + pp].ap()[:], 1.0 / 127.0, g_tmp.ap()[:],
                        ALU.mult, ALU.mult), VE)
                    bank_free[2 + pp] = hm
                    state["prev_hm"] = hm
                    do("vector", lambda e, ht=ht: e.tensor_copy(hspv[:, ht, 0, :], h_tmp.ap()[:]), VE)
                    do("vector", lambda e, ht=ht: e.tensor_sub(h_tmp.ap()[:], h_tmp.ap()[:], hspv[:, ht, 0, :]), VE)
                    do("vector", lambda e, ht=ht: e.tensor_copy(hspv[:, ht, 1, :], h_tmp.ap()[:]), VE)
                    do("vector", lambda e, ht=ht: e.tensor_sub(h_tmp.ap()[:], h_tmp.ap()[:], hspv[:, ht, 1, :]), VE)
                    hsp_tok[ht] = do("vector", lambda e, ht=ht: e.tensor_copy(hspv[:, ht, 2, :], h_tmp.ap()[:]), VE)
                next_ln = None
                for dt in range(DC):
                    for third in range(3):
                        wait("sync", conv_ready.get(("d", dt)))
                        ds[third] = dma_w(
                            wdslab[:, third, :, :],
                            wdb[dt].rearrange("p (c j) -> p c j", c=HC)[:, third * 8:(third + 1) * 8, :],
                            df[third])
                    ppd = dt % 2
                    pe_writes_bank(4 + ppd)
                    for third in range(3):
                        wait("tensor", hsp_tok[third * 8 + 7])
                        wait("tensor", ds[third])
                        mm = None
                        for cl in range(8):
                            c = third * 8 + cl
                            for s_i in range(3):
                                last_third = (cl == 7 and s_i == 2)
                                last_all = (c == HC - 1 and s_i == 2)
                                mm = do("tensor", lambda e, third=third, cl=cl, c=c, s_i=s_i, ppd=ppd, last_all=last_all: e.matmul(
                                    pb[4 + ppd].ap()[:], wdslab[:, third, cl, :], hspv[:, c, s_i, :],
                                    start=(c == 0 and s_i == 0), stop=last_all),
                                    PE if last_third else None)
                        df[third] = mm
                    down_mm = mm
                    wait("vector", down_mm)
                    r_written = do("vector", lambda e, dt=dt, t0=t0, ppd=ppd: e.scalar_tensor_tensor(
                        Rv[:, dt, t0:t0 + T], pb[4 + ppd].ap()[:], 0.5 / 127.0, Rv[:, dt, t0:t0 + T],
                        ALU.mult, ALU.add), VE)
                    bank_free[4 + ppd] = r_written
                    if INTERLEAVE and tt + 1 < NT:
                        if dt == 0:
                            next_ln = ln_tile(b, tt + 1, "ffn", None, xsqv, arep.ap(), brep.ap())
                            state["next_repl"] = next_ln()
                        else:
                            tAr2, tBr2 = state["next_repl"]
                            split_done = split_chunk(b, tt + 1, dt - 1, tAr2, tBr2,
                                                     state.get("xsp_free"))
                            if dt == DC - 1:
                                for c2 in range(DC - 1, DC):
                                    split_done = split_chunk(b, tt + 1, c2, tAr2, tBr2,
                                                             state.get("xsp_free"))
            return r_written

        def attn_sublayer(b, r_tok):
            wait("tensor", sm_tok)
            wait("vector", sm_tok)
            wait("sync", r_tok)
            awtok = None
            for nm, src in (("i", wi_t), ("f", wf_t), ("z", wz_t), ("o", wo_t)):
                awtok = do("sync", lambda e, nm=nm, src=src: e.dma_start(
                    out=awv[nm][:], in_=src.rearrange("c p n -> p c n")[:]), SIO, dma=True)
            awtok = do("sync", lambda e: e.dma_start(
                out=awp_v[:], in_=wp_t.rearrange("c p j -> p c j")[:]), SIO, dma=True)
            wait("tensor", awtok)
            prev_evac = [None, None, None, None]
            stats_read = None
            xn_pe = [None, None]
            for tt in range(NT):
                t0 = tt * T
                emit_repl = ln_tile(b, tt, "attn", r_tok if tt == 0 else None,
                                    axsqv, aarep.ap(), abrep.ap())
                tAr, tBr = emit_repl()
                mm_tok = [None, None, None, None]
                for c in range(DC):
                    pp = c % 2
                    wait("vector", tAr)
                    wait("vector", tBr)
                    wait("vector", xn_pe[pp])
                    do("vector", lambda e, c=c, t0=t0, pp=pp: e.tensor_mul(
                        xn2v[:, pp, :], Rv[:, c, t0:t0 + T], aarep.ap()[:]), VE)
                    do("vector", lambda e, pp=pp: e.tensor_add(xn2v[:, pp, :], xn2v[:, pp, :], abrep.ap()[:]), VE)
                    xt = do("vector", lambda e, c=c, pp=pp: e.tensor_scalar(
                        xn2v[:, pp, :], xn2v[:, pp, :], sb_lnw.ap()[:, c:c + 1], sb_lnb.ap()[:, c:c + 1],
                        ALU.mult, ALU.add), VE)
                    wait("tensor", xt)
                    for j, nm in enumerate(("i", "f", "z", "o")):
                        if c == 0:
                            pe_writes_bank(j)
                        last = (c == DC - 1)
                        tok = do("tensor", lambda e, nm=nm, j=j, c=c, pp=pp: e.matmul(
                            pb[j].ap()[0:NL, :], awv[nm][:, c, :], xn2v[:, pp, :],
                            start=(c == 0), stop=last),
                            PE if (last or nm == "o") else None)
                        if nm == "o":
                            xn_pe[pp] = tok
                        if last:
                            mm_tok[j] = tok
                wait("vector", mm_tok[0])
                do("vector", lambda e, t0=t0: e.tensor_scalar(
                    sc_li.ap()[:, t0:t0 + T], pb[0].ap()[0:NL, :], sb_b["i"].ap()[:, 0:1], -20.0,
                    ALU.add, ALU.max), VE)
                prev_evac[0] = do("vector", lambda e, t0=t0: e.tensor_scalar(
                    sc_li.ap()[:, t0:t0 + T], sc_li.ap()[:, t0:t0 + T], 20.0, None, ALU.min), VE)
                bank_free[0] = prev_evac[0]
                wait("vector", mm_tok[1])
                do("vector", lambda e, t0=t0: e.tensor_scalar(
                    sc_lf.ap()[:, t0:t0 + T], pb[1].ap()[0:NL, :], sb_b["f"].ap()[:, 0:1], -20.0,
                    ALU.add, ALU.max), VE)
                prev_evac[1] = do("vector", lambda e, t0=t0: e.tensor_scalar(
                    sc_lf.ap()[:, t0:t0 + T], sc_lf.ap()[:, t0:t0 + T], 20.0, None, ALU.min), VE)
                bank_free[1] = prev_evac[1]
                wait("vector", mm_tok[2])
                prev_evac[2] = do("vector", lambda e, t0=t0: e.tensor_scalar(
                    sc_z.ap()[:, t0:t0 + T], pb[2].ap()[0:NL, :], sb_b["z"].ap()[:, 0:1], None, ALU.add), VE)
                bank_free[2] = prev_evac[2]
                wait("vector", mm_tok[3])
                prev_evac[3] = do("vector", lambda e, t0=t0: e.tensor_scalar(
                    sc_o.ap()[:, t0:t0 + T], pb[3].ap()[0:NL, :], sb_b["o"].ap()[:, 0:1], None, ALU.add), VE)
                bank_free[3] = prev_evac[3]
            wait("scalar", prev_evac[2])
            t_z = do("scalar", lambda e: e.activation(sc_z.ap()[:], sc_z.ap()[:], ACT.Tanh), SE)
            do("vector", lambda e: e.memset(sc_m.ap()[:, 0:1], 0.0), VE)
            wait("vector", prev_evac[0])
            wait("vector", prev_evac[1])
            t_m = do("vector", lambda e: e.tensor_tensor_scan(
                sc_m.ap()[:, 1:S + 1], sc_lf.ap()[:], sc_li.ap()[:], 0.0, ALU.add, ALU.max), VE)
            do("vector", lambda e: e.tensor_sub(sc_lf.ap()[:], sc_lf.ap()[:], sc_m.ap()[:, 1:S + 1]), VE)
            t_a = do("vector", lambda e: e.tensor_add(sc_lf.ap()[:], sc_lf.ap()[:], sc_m.ap()[:, 0:S]), VE)
            wait("scalar", t_a)
            t_fs = do("scalar", lambda e: e.activation(sc_lf.ap()[:], sc_lf.ap()[:], ACT.Exp), SE)
            t_bv = do("vector", lambda e: e.tensor_sub(sc_li.ap()[:], sc_li.ap()[:], sc_m.ap()[:, 1:S + 1]), VE)
            wait("scalar", t_bv)
            t_is = do("scalar", lambda e: e.activation(sc_li.ap()[:], sc_li.ap()[:], ACT.Exp), SE)
            wait("vector", t_is)
            t_iz = do("vector", lambda e: e.tensor_mul(sc_z.ap()[:], sc_li.ap()[:], sc_z.ap()[:]), VE)
            wait("vector", t_fs)
            t_c = do("vector", lambda e: e.tensor_tensor_scan(
                sc_m.ap()[:, 0:S], sc_lf.ap()[:], sc_z.ap()[:], 0.0, ALU.mult, ALU.add), VE)
            wait("scalar", t_c)
            t_tc = do("scalar", lambda e: e.activation(sc_m.ap()[:, 0:S], sc_m.ap()[:, 0:S], ACT.Tanh), SE)
            wait("scalar", prev_evac[3])
            t_o = do("scalar", lambda e: e.activation(sc_o.ap()[:], sc_o.ap()[:], ACT.Sigmoid), SE)
            wait("vector", t_o)
            t_h = do("vector", lambda e: e.tensor_mul(sc_o.ap()[:], sc_o.ap()[:], sc_m.ap()[:, 0:S]), VE)
            r_written = None
            wait("tensor", t_h)
            for tt in range(NT):
                t0 = tt * T
                for dt in range(DC):
                    ppd = (tt * DC + dt) % 2
                    pe_writes_bank(4 + ppd)
                    mo = do("tensor", lambda e, dt=dt, t0=t0, ppd=ppd: e.matmul(
                        pb[4 + ppd].ap()[:], awp_v[:, dt, :], sc_o.ap()[:, t0:t0 + T],
                        start=True, stop=True), PE)
                    wait("vector", mo)
                    r_written = do("vector", lambda e, dt=dt, t0=t0, ppd=ppd: e.scalar_tensor_tensor(
                        Rv[:, dt, t0:t0 + T], pb[4 + ppd].ap()[:], sb_bp.ap()[:, dt:dt + 1],
                        Rv[:, dt, t0:t0 + T], ALU.add, ALU.add), VE)
                    bank_free[4 + ppd] = r_written
            return r_written

        last_out = [None]

        def phase_out(b, r_tok):
            wait("tensor", r_tok)
            for sub in range(NSUB):
                p = sub % 2
                t0 = sub * 128
                evs = []
                for half in range(2):
                    bi = 6 + half
                    pe_writes_bank(bi)
                    tr = None
                    for c4 in range(4):
                        c = half * 4 + c4
                        tr = do("tensor", lambda e, c=c, c4=c4, t0=t0, bi=bi: e.transpose(
                            pb[bi].ap()[:, c4 * 128:(c4 + 1) * 128],
                            Rv[:, c, t0:t0 + 128], ident.ap()[:]),
                            PE if c4 == 3 else None)
                    wait("vector", tr)
                    wait("vector", io_free[p])
                    ev = do("vector", lambda e, p=p, half=half, bi=bi: e.tensor_copy(
                        iov[:, p, half * 512:(half + 1) * 512], pb[bi].ap()[:]), VE)
                    bank_free[bi] = ev
                    evs.append(ev)
                wait("sync", evs[0])
                wait("sync", evs[1])
                st = do("sync", lambda e, b=b, t0=t0, p=p: e.dma_start(
                    out=y_out[b, t0:t0 + 128, :], in_=iov[:, p, :]), DOUT, dma=True)
                io_free[p] = st
            last_out[0] = st

        conv_pre = None
        conv_last = conv_ready[("u", HC - 1)]
        wait("sync", conv_last)
        wait("vector", conv_last)
        wait("scalar", conv_last)
        wait("tensor", conv_last)
        for b in range(n_batches):
            r_tok = phase_in(b)
            state["xsp_free"] = None
            r_tok = ffn_sublayer(b, r_tok, conv_pre)
            r_tok = attn_sublayer(b, r_tok)
            state["xsp_free"] = None
            r_tok = ffn_sublayer(b, r_tok, conv_pre)
            phase_out(b, r_tok)
        do("sync", lambda e: e.wait_ge(DOUT, sch.semv[DOUT]))
        sch.emit(block)
    return nc




def kernel(**inputs):
    x = np.ascontiguousarray(np.asarray(inputs["L0_attn_x"], dtype=np.float32))
    n_cores = 8
    prep = host_prepare(inputs)
    nc = build_program(n_batches=B_CORE)
    in_maps = []
    for c in range(n_cores):
        m = {"x_in": np.ascontiguousarray(x[c * B_CORE:(c + 1) * B_CORE])}
        m.update(prep)
        in_maps.append(m)
    from concourse.bass_utils import run_bass_kernel_spmd
    res = run_bass_kernel_spmd(nc, in_maps, core_ids=list(range(n_cores)))
    return np.concatenate([r["y_out"] for r in res.results], axis=0)
